# version 02 (final)
# baseline (speedup 1.0000x reference)
import os
from contextlib import contextmanager

import numpy as np
import ml_dtypes
import concourse.bass as bass
import concourse.mybir as mybir
from concourse.expressions import RuntimeValue
from concourse.ordered_engine_set import OrderedEngineSet
from concourse.bass_utils import run_bass_kernel_spmd

F32 = mybir.dt.float32
BF16 = mybir.dt.bfloat16
AF = mybir.ActivationFunctionType
ALU = mybir.AluOpType
AXX = mybir.AxisListType.X
ts = bass.ts
ds = bass.ds

D = 2048
B = 2
S = 2048
T = B * S
ND = D // 128
SCALE_A = float(1.0 / np.sqrt(128.0))

_uid = [0]


def _tag():
    _uid[0] += 1
    return f"u{_uid[0]}"


class LV:
    def __init__(self, reg, lo, hi):
        self.reg, self.lo, self.hi = reg, lo, hi
        self.rv = RuntimeValue(reg, min_val=lo, max_val=hi)


def axv(lv, a, b):
    if isinstance(lv, int):
        return a * lv + b
    e = lv.rv * a
    return e + b if b else e


@contextmanager
def fori(eng, start, end):
    nc = eng.bass
    tag = _tag()
    loop, endl = f"{tag}_L", f"{tag}_E"
    engines = OrderedEngineSet([eng.engine])
    with eng.register(f"{tag}_i") as r:
        eng.reg_mov(r, start)
        eng.br(loop)
        with nc.body(loop, valid_engines=engines):
            yield LV(r, start, end - 1)
            eng.reg_alu(r, r, 1, ALU.add)
            eng.br_lt(r, end, on_true=loop, on_false=endl)
        nc.switch_bb(endl)


def wval(eng, reg, lv, a, b):
    if isinstance(lv, int):
        return a * lv + b
    eng.reg_alu(reg, lv.reg, a, ALU.mult)
    if b:
        eng.reg_alu(reg, reg, b, ALU.add)
    return RuntimeValue(reg)


def wge(eng, sm, val):
    if isinstance(val, int):
        if val > 0:
            eng.wait_ge(sm, val)
    else:
        eng.wait_ge(sm, val)


def mkchain(p, eng):
    sem2 = p.sem()
    n = [0]

    def step(inst=None):
        n[0] += 1
        (inst if inst is not None else eng.engine_nop()).then_inc(sem2, 1)
        eng.wait_ge(sem2, n[0])
    return step


class Regs:
    def __init__(self, eng, n):
        self.eng, self.n = eng, n
        self.cms, self.rs = [], []

    def __enter__(self):
        t = _tag()
        for k in range(self.n):
            cm = self.eng.register(f"{t}_r{k}")
            self.cms.append(cm)
            self.rs.append(cm.__enter__())
        return self.rs

    def __exit__(self, *a):
        for cm in reversed(self.cms):
            cm.__exit__(*a)
        return False


class P:
    pass


def build_program():
    nc = bass.Bass("TRN2", target_bir_lowering=False,
                   detect_race_conditions=os.environ.get("KRACE", "1") == "1")
    p = P()
    p.nc = nc
    ap = lambda *a, **k: nc.dram_tensor(*a, **k).ap()
    p.xT = ap("xT", [D, T], F32, kind="ExternalInput")
    p.wqkv = ap("wqkv", [48 * 128, ND * 128], F32, kind="ExternalInput")
    p.wo_a = ap("wo_a", [16 * 128, ND * 128], F32, kind="ExternalInput")
    p.res1 = ap("res1", [D, T], F32, kind="ExternalOutput")
    p.res2 = ap("res2", [D, T], F32, kind="ExternalOutput")
    p.qkvA = ap("qkvA", [6144, T], BF16, kind="Internal")
    p.shT = ap("shT", [2048, 2048], BF16, kind="Internal")
    p.wrt = ap("wrt", [128, ND * 128], F32, kind="ExternalInput")
    p.wgu = ap("wgu", [64 * 128, 2 * ND * 128], F32, kind="ExternalInput")
    p.wds = ap("wds", [16 * 128, 24 * 128], F32, kind="ExternalInput")
    p.swg = ap("swg", [16 * 128, ND * 128], F32, kind="ExternalInput")
    p.swu = ap("swu", [16 * 128, ND * 128], F32, kind="ExternalInput")
    p.amp = ap("amp", [128, 16], F32, kind="ExternalInput")
    p.esc = ap("esc", [128, 1], F32, kind="ExternalInput")
    p.wqkvB = ap("wqkvB", [24 * 128, ND * 128], F32, kind="ExternalInput")
    p.wo_b = ap("wo_b", [16 * 128, ND * 128], F32, kind="ExternalInput")
    p.res3 = ap("res3", [D, T], F32, kind="ExternalOutput")
    p.res4 = ap("res4", [D, T], F32, kind="ExternalOutput")
    p.qfT = ap("qfT", [2048, T], BF16, kind="Internal")
    p.flaT = ap("flaT", [2048, T], BF16, kind="Internal")

    p._ctx = []
    p._sembank = []
    p._semidx = [0]

    def sem():
        i = p._semidx[0]
        while len(p._sembank) <= i:
            cm = nc.semaphore()
            s = cm.__enter__()
            p._ctx.append(cm)
            p._sembank.append(s)
        p._semidx[0] = i + 1
        assert i < 40, "sem pool exhausted within one phase"
        return p._sembank[i]
    p.sem = sem

    def dsem():
        cm = nc.semaphore()
        s = cm.__enter__()
        p._ctx.append(cm)
        return s
    p.dsem = dsem

    _barcm = nc.semaphore()
    p._bar = _barcm.__enter__()
    p._ctx.append(_barcm)
    p._barcnt = [0]

    def semreset():
        used = p._sembank[:p._semidx[0]]
        if not used:
            return
        bar, cnt = p._bar, p._barcnt[0]
        with nc.Block() as blk:
            @blk.tensor
            def _(t):
                t.nop().then_inc(bar, 1)
                t.wait_ge(bar, cnt + 5)

            @blk.vector
            def _(v):
                v.nop().then_inc(bar, 1)
                v.wait_ge(bar, cnt + 5)

            @blk.scalar
            def _(sc):
                sc.nop().then_inc(bar, 1)
                sc.wait_ge(bar, cnt + 5)

            @blk.sync
            def _(sy):
                sy.nop().then_inc(bar, 1)
                sy.wait_ge(bar, cnt + 5)

            @blk.gpsimd
            def _(g):
                g.wait_ge(bar, cnt + 4)
                for s in used:
                    g.sem_clear(s)
                g.nop().then_inc(bar, 1)
        p._barcnt[0] = cnt + 5
        p._semidx[0] = 0
    p.semreset = semreset

    def sb(name, shape, dt):
        cm = nc.sbuf_tensor(name, shape, dt)
        t = cm.__enter__()
        p._ctx.append(cm)
        return t

    p.HB = sb("HB", [128, 16 * T], BF16)
    p.WK = sb("WK", [128, 19456], BF16)
    p.WF = sb("WF", [128, 4096], F32)
    p.R = sb("R", [1, 2560], F32)
    p.CONST = sb("CONST", [128, 512], BF16)
    p.ONES1F = sb("ONES1F", [1, 128], F32)

    p.IDENT = p.CONST[:, 0:128]
    p.TRIU = p.CONST[:, 128:256]
    p.ONESC = p.CONST[:, 256:257]
    p.MIXP = p.CONST[:, 260:268]
    p.ONES1B = p.CONST[0:1, 384:512]
    p.CF = sb("CF", [128, 128], F32)
    p.C8 = sb("C8", [8, 1024], BF16)
    p.CONST2 = sb("CONST2", [128, 768], BF16)
    p.TRIU4 = p.CONST2[:, 0:512]
    p.BD2 = p.CONST2[:, 512:514]
    p.BD2T = p.CONST2[0:2, 514:642]
    p.CFB = sb("CFB", [128, 68], F32)
    p.FONE = p.CFB[:, 0:1]
    p.FLNC = p.CFB[:, 1:2]
    p.C99 = p.CFB[0:1, 4:68]

    build_consts(p)
    phase_prep_mix(p)
    nph = int(os.environ.get("KPHASES", "99"))
    p.semreset()
    phase_ln(p, src=p.xT)
    if nph >= 2:
        p.semreset()
        phase_proj(p, p.wqkv, 48, p.qkvA, "store_bf16")
    if nph >= 3:
        p.semreset()
        phase_attn_a(p)
    if nph >= 4:
        p.semreset()
        phase_proj(p, p.wo_a, 16, p.res1, "resid", res_in=p.xT)
    if nph >= 5:
        phase_ffn(p, p.res1, p.res2)
    if nph >= 6:
        p.semreset()
        phase_ln(p, src=p.res2)
        p.semreset()
        phase_proj(p, p.wqkvB, 24, p.qkvA, "store_bf16")
    if nph >= 7:
        p.semreset()
        phase_bprep(p)
    if nph >= 8:
        p.semreset()
        phase_fla(p)
    if nph >= 9:
        p.semreset()
        phase_attn_b(p)
        p.semreset()
        phase_proj(p, p.wo_b, 16, p.res3, "resid_fla", res_in=p.res2)
    if nph >= 10:
        phase_ffn(p, p.res3, p.res4)
    if nph < 4:
        phase_dump_hb(p)
    return p


def phase_dump_hb(p):
    nc = p.nc
    s_st = p.dsem()
    with nc.Block() as blk:
        @blk.gpsimd
        def _(g):
            for dc in range(ND):
                g.dma_start(out=p.res1[dc * 128:(dc + 1) * 128, :],
                            in_=p.HB[:, dc * T:(dc + 1) * T]).then_inc(s_st, 16)
            g.wait_ge(s_st, ND * 16)


def build_consts(p):
    nc = p.nc
    p._csem = p.dsem()
    with nc.Block() as blk:
        @blk.gpsimd
        def _(g):
            g.memset(p.CONST[:, :], 0.0)
            g.affine_select(out=p.IDENT, in_=p.IDENT, compare_op=ALU.not_equal,
                            fill=1.0, base=0, pattern=[[-1, 128]],
                            channel_multiplier=1)
            g.affine_select(out=p.TRIU, in_=p.TRIU, compare_op=ALU.is_gt,
                            fill=1.0, base=0, pattern=[[-1, 128]],
                            channel_multiplier=1)
            g.memset(p.ONESC, 1.0)
            g.memset(p.ONES1B, 1.0)
            g.memset(p.ONES1F[:, :], 1.0)
            g.memset(p.CF[:, :], 0.0)
            g.affine_select(out=p.CF[:, :], in_=p.CF[:, :],
                            compare_op=ALU.not_equal, fill=1.0, base=0,
                            pattern=[[-1, 128]], channel_multiplier=1)
            g.memset(p.C8[:, :], 0.0)
            g.affine_select(out=p.C8[:, :], in_=p.C8[:, :],
                            compare_op=ALU.not_equal, fill=1.0, base=0,
                            pattern=[[-1, 8], [0, 128]], channel_multiplier=1)
            g.memset(p.CONST2[:, :], 0.0)
            g.affine_select(out=p.TRIU4.rearrange("p (r c) -> p r c", r=4),
                            in_=p.TRIU4.rearrange("p (r c) -> p r c", r=4),
                            compare_op=ALU.is_gt, fill=1.0, base=0,
                            pattern=[[0, 4], [-1, 128]], channel_multiplier=1)
            g.memset(p.CONST2[0:64, 512:513], 1.0)
            g.memset(p.CONST2[64:128, 513:514], 1.0)
            g.memset(p.CFB[:, :], 0.0)
            g.memset(p.FONE, 1.0)
            g.memset(p.FLNC, float(np.log(0.00125)))
            g.memset(p.C99, 0.99).then_inc(p._csem, 1)

    with (
        nc.psum_tensor("pBD" + _tag(), [2, 128], BF16) as pBD,
        nc.Block() as blk,
    ):
        @blk.tensor
        def _(t):
            t.wait_ge(p._csem, 1)
            t.transpose(pBD[:, :], p.BD2, p.IDENT).then_inc(p._csem, 1)

        @blk.vector
        def _(v):
            v.wait_ge(p._csem, 2)
            v.tensor_copy(p.BD2T, pBD[:, :]).then_inc(p._csem, 1)

        @blk.gpsimd
        def _(g):
            g.wait_ge(p._csem, 3)


def phase_ln(p, src, col0=0, tw=T):
    nc = p.nc
    s_ld, s_sq, s_mq = p.dsem(), p.sem(), p.sem()
    s_ev, s_row, s_rst, s_bc, s_bce = (p.sem() for _ in range(5))
    sqs = [p.WK[:, 0:1024], p.WK[:, 1024:2048]]
    mu_r = [p.R[0:1, 0:512], p.R[0:1, 512:1024]]
    rs_r = [p.R[0:1, 1024:1536], p.R[0:1, 1536:2048]]
    m2_r = p.R[0:1, 2048:2560]
    mubc = p.WF[:, 0:2048]
    rsbc = p.WF[:, 2048:4096]

    with (
        nc.psum_tensor("psS" + _tag(), [1, 1024], F32) as psS,
        nc.psum_tensor("psQ" + _tag(), [1, 1024], F32) as psQ,
        nc.psum_tensor("psB0" + _tag(), [128, 512], F32) as psB0,
        nc.psum_tensor("psB1" + _tag(), [128, 512], F32) as psB1,
        nc.Block() as blk,
    ):
        @blk.gpsimd
        def _(g):
            for dc in range(ND):
                g.dma_start(out=p.HB[:, dc * tw:(dc + 1) * tw],
                            in_=src[dc * 128:(dc + 1) * 128, col0:col0 + tw]) \
                    .then_inc(s_ld, 16)

        @blk.scalar
        def _(sc):
            def rstd_q(qt):
                for tc2 in range(2):
                    gr = qt * 2 + tc2
                    sc.wait_ge(s_row, gr + 1)
                    r = rs_r[tc2]
                    sc.activation(r, r, AF.Ln)
                    sc.activation(r, r, AF.Exp, scale=-0.5).then_inc(s_rst, 1)
            for qt in range(tw // 1024):
                for dc in range(ND):
                    gsq = qt * ND + dc
                    sc.wait_ge(s_ld, 16 * (dc + 1))
                    wge(sc, s_mq, gsq - 1)
                    sc.activation(sqs[dc % 2][:, :],
                                  p.HB[:, dc * tw + qt * 1024:
                                       dc * tw + qt * 1024 + 1024],
                                  AF.Square).then_inc(s_sq, 1)
                rstd_q(qt)

        @blk.tensor
        def _(t):
            for qt in range(tw // 1024):
                for dc in range(ND):
                    gsq = qt * ND + dc
                    t.wait_ge(s_sq, gsq + 1)
                    if dc == 0:
                        wge(t, s_ev, qt)
                    for c in range(2):
                        t.matmul(psS[:, c * 512:(c + 1) * 512], p.ONESC,
                                 p.HB[:, dc * tw + qt * 1024 + c * 512:
                                      dc * tw + qt * 1024 + (c + 1) * 512],
                                 start=(dc == 0), stop=(dc == ND - 1))
                        mm = t.matmul(psQ[:, c * 512:(c + 1) * 512], p.ONESC,
                                      sqs[dc % 2][:, c * 512:(c + 1) * 512],
                                      start=(dc == 0), stop=(dc == ND - 1))
                    mm.then_inc(s_mq, 1)
                for tc2 in range(2):
                    gr = qt * 2 + tc2
                    t.wait_ge(s_rst, gr + 1)
                    wge(t, s_bce, gr)
                    t.matmul(psB0[:, :], p.ONES1F[:, :], mu_r[tc2],
                             start=True, stop=True)
                    t.matmul(psB1[:, :], p.ONES1F[:, :], rs_r[tc2],
                             start=True, stop=True).then_inc(s_bc, 1)

        @blk.vector
        def _(v):
            for qt in range(tw // 1024):
                th, sh = qt // 2, qt % 2
                v.wait_ge(s_mq, (qt + 1) * ND)
                for tc2 in range(2):
                    gr = qt * 2 + tc2
                    wge(v, s_bc, gr - 1)
                    v.tensor_scalar_mul(mu_r[tc2], psS[:, ts(tc2, 512)],
                                        1.0 / D)
                    mq = v.tensor_scalar_mul(m2_r, psQ[:, ts(tc2, 512)],
                                             1.0 / D)
                    if tc2 == 1:
                        mq.then_inc(s_ev, 1)
                    v.tensor_mul(rs_r[tc2], mu_r[tc2], mu_r[tc2])
                    v.tensor_sub(rs_r[tc2], m2_r, rs_r[tc2])
                    v.tensor_scalar_add(rs_r[tc2], rs_r[tc2], 1e-5) \
                        .then_inc(s_row, 1)
                for tc2 in range(2):
                    gr = qt * 2 + tc2
                    v.wait_ge(s_bc, gr + 1)
                    o0 = sh * 1024 + tc2 * 512
                    v.tensor_copy(mubc[:, o0:o0 + 512], psB0[:, :])
                    v.tensor_copy(rsbc[:, o0:o0 + 512], psB1[:, :]) \
                        .then_inc(s_bce, 1)
                if sh == 1:
                    for dc in range(ND):
                        sl = p.HB[:, dc * tw + th * 2048:
                                  dc * tw + th * 2048 + 2048]
                        v.tensor_sub(sl, sl, mubc)
                        v.tensor_mul(sl, sl, rsbc)


def phase_proj(p, wsrc, NO, dst, mode, res_in=None):
    nc = p.nc
    s_w, s_st, s_x = p.dsem(), p.dsem(), p.dsem()
    s_pe, s_ev = p.sem(), p.sem()
    wslots = [p.WK[:, 0:2048], p.WK[:, 2048:4096]]
    if mode == "store_bf16":
        eslots = [p.WK[:, 4096:6144], p.WK[:, 6144:8192]]
    else:
        xslot = p.WF[:, 0:2048]
        eslot = p.WF[:, 2048:4096]
        fslot = p.WK[:, 4096:6144]

    with (
        nc.psum_tensor("psA" + _tag(), [128, 2048], F32) as psA,
        nc.psum_tensor("psB" + _tag(), [128, 2048], F32) as psB,
        nc.Block() as blk,
    ):
        pq = [psA[:, 0:1024], psA[:, 1024:2048],
              psB[:, 0:1024], psB[:, 1024:2048]]

        @blk.gpsimd
        def _(g):
            for oc in range(NO):
                wge(g, s_pe, 4 * oc - 4)
                g.dma_start(out=wslots[oc % 2][:, :],
                            in_=wsrc[oc * 128:(oc + 1) * 128, :]) \
                    .then_inc(s_w, 16)

        @blk.tensor
        def _(t):
            with Regs(t, 1) as (rw,):
                def body(i, first):
                    for pp in range(2):
                        t.wait_ge(s_w, wval(t, rw, i, 32, 16 * pp + 16))
                        for q in range(4):
                            wge(t, s_ev, wval(t, rw, i, 8, 4 * pp + q - 3))
                            for dc in range(ND):
                                for c in range(2):
                                    mm = t.matmul(
                                        pq[q][:, c * 512:(c + 1) * 512],
                                        wslots[pp][:, ts(dc, 128)],
                                        p.HB[:, dc * T + q * 1024 + c * 512:
                                             dc * T + q * 1024
                                             + (c + 1) * 512],
                                        start=(dc == 0), stop=(dc == ND - 1))
                            mm.then_inc(s_pe, 1)
                body(0, True)
                with fori(t, 1, NO // 2) as i:
                    body(i, False)

        @blk.vector
        def _(v):
            with Regs(v, 1) as (rw,):
                def body(i, first):
                    for k in range(8):
                        pp, q = k // 4, k % 4
                        b, qq = q // 2, q % 2
                        v.wait_ge(s_pe, wval(v, rw, i, 8, k + 1))
                        if mode == "store_bf16":
                            wge(v, s_st, wval(v, rw, i, 64,
                                              32 * pp + 16 * b - 16))
                            v.tensor_copy(eslots[b][:, qq * 1024:
                                                    (qq + 1) * 1024],
                                          pq[q][:, :]).then_inc(s_ev, 1)
                        else:
                            xs = 128 if mode == "resid_fla" else 64
                            v.wait_ge(s_x, wval(v, rw, i, xs,
                                                (32 * pp + 16 * b + 16)
                                                * (xs // 64)))
                            wge(v, s_st, wval(v, rw, i, 64,
                                              32 * pp + 16 * b))
                            esl = eslot[:, qq * 1024:(qq + 1) * 1024]
                            if mode == "resid_fla":
                                v.tensor_scalar_mul(esl, pq[q][:, :], 0.99)
                                v.tensor_add(
                                    esl, esl,
                                    xslot[:, qq * 1024:(qq + 1) * 1024])
                                v.tensor_add(
                                    esl, esl,
                                    fslot[:, qq * 1024:(qq + 1) * 1024]) \
                                    .then_inc(s_ev, 1)
                            else:
                                v.tensor_add(
                                    esl, pq[q][:, :],
                                    xslot[:, qq * 1024:(qq + 1) * 1024]) \
                                    .then_inc(s_ev, 1)
                body(0, True)
                with fori(v, 1, NO // 2) as i:
                    body(i, False)

        @blk.sync
        def _(sy):
            for oc in range(NO):
                for b in range(2):
                    u = 2 * oc + b
                    if mode in ("resid", "resid_fla"):
                        wge(sy, s_ev, 2 * u)
                        sy.dma_start(
                            out=xslot[:, :],
                            in_=res_in[oc * 128:(oc + 1) * 128,
                                       b * 2048:(b + 1) * 2048]) \
                            .then_inc(s_x, 16)
                        if mode == "resid_fla":
                            sy.dma_start(
                                out=fslot[:, :],
                                in_=p.flaT[oc * 128:(oc + 1) * 128,
                                           b * 2048:(b + 1) * 2048]) \
                                .then_inc(s_x, 16)
                    sy.wait_ge(s_ev, 2 * u + 2)
                    src_slot = (eslots[b] if mode == "store_bf16" else eslot)
                    sy.dma_start(out=dst[oc * 128:(oc + 1) * 128,
                                         b * 2048:(b + 1) * 2048],
                                 in_=src_slot[:, :]).then_inc(s_st, 16)
            sy.wait_ge(s_st, NO * 2 * 16)


def phase_attn_a(p):
    nc = p.nc
    s_ld = p.dsem()
    (s_vt, s_vte, s_sc, s_ex, s_msk, s_pvg, s_rcp, s_rbc,
     s_oev, s_oq) = (p.sem() for _ in range(10))
    qs = [p.WK[:, 0:2048], p.WK[:, 6144:8192]]
    ks = [p.WK[:, 2048:4096], p.WK[:, 8192:10240]]
    vs = [p.WK[:, 4096:6144], p.WK[:, 10240:12288]]
    vtok = p.WK[:, 12288:14336]
    sqb = p.WK[:, 14336:16384]
    Eb = [p.WK[:, 16384:16896], p.WK[:, 16896:17408]]
    recip = p.R[0:1, 1536:2048]

    groups = []
    for qc in range(4):
        for j in range(4 * qc + 4):
            jj = j - 4 * qc
            groups.append((qc, j, 128 * jj if jj >= 0 else 0))
    NG = len(groups)
    G_end = [4, 12, 24, 40]

    with (
        nc.psum_tensor("psOut" + _tag(), [128, 512], F32) as psOut,
        nc.psum_tensor("psL" + _tag(), [1, 512], F32) as psL,
        nc.psum_tensor("psS0" + _tag(), [128, 512], F32) as psS0,
        nc.psum_tensor("psS1" + _tag(), [128, 512], F32) as psS1,
        nc.psum_tensor("psT" + _tag(), [128, 1024], BF16) as psT,
        nc.psum_tensor("psR" + _tag(), [128, 512], F32) as psR,
        nc.Block() as blk,
    ):
        psSc = [psS0, psS1]

        @blk.sync
        def _(sy):
            for b in range(2):
                for h in range(16):
                    hl = b * 16 + h
                    wge(sy, s_oq, 4 * hl - 4)
                    sy.dma_start(
                        out=p.WK[:, (h % 2) * 6144:(h % 2) * 6144 + 6144]
                        .rearrange("p (g s) -> p g s", g=3),
                        in_=p.qkvA[h * 384:(h + 1) * 384,
                                   b * 2048:(b + 1) * 2048]
                        .rearrange("(g p) s -> p g s", p=128)) \
                        .then_inc(s_ld, 16)

        @blk.scalar
        def _(sc):
            with Regs(sc, 1) as (rw,):
                for b in range(2):
                    def body(i, first):
                        for pp in range(2):
                            for g in range(NG):
                                qc, j, c0 = groups[g]
                                sc.wait_ge(s_sc, wval(sc, rw, i, 80,
                                                      640 * b + 40 * pp
                                                      + g + 1))
                                wge(sc, s_pvg, wval(sc, rw, i, 80,
                                                    640 * b + 40 * pp
                                                    + g - 1))
                                sc.activation(Eb[g % 2][:, c0:512],
                                              psSc[g % 2][:, c0:512],
                                              AF.Exp, scale=SCALE_A) \
                                    .then_inc(s_ex, 1)
                    body(0, True)
                    with fori(sc, 1, 8) as i:
                        body(i, False)

        @blk.vector
        def _(v):
            with Regs(v, 1) as (rw,):
                for b in range(2):
                    def body(i, first):
                        for pp in range(2):
                            for half in range(2):
                                v.wait_ge(s_vt, wval(v, rw, i, 4,
                                                     32 * b + 2 * pp
                                                     + half + 1))
                                if half == 0:
                                    wge(v, s_pvg, wval(v, rw, i, 80,
                                                       640 * b + 40 * pp))
                                v.tensor_copy(
                                    vtok[:, half * 1024:(half + 1) * 1024],
                                    psT[:, :]).then_inc(s_vte, 1)
                            wge(v, s_oev, wval(v, rw, i, 2, 16 * b + pp))
                            for qc in range(4):
                                for g in range(G_end[qc] - 4, G_end[qc]):
                                    _qc, j, c0 = groups[g]
                                    v.wait_ge(s_ex, wval(v, rw, i, 80,
                                                         640 * b + 40 * pp
                                                         + g + 1))
                                    v.tensor_mul(Eb[g % 2][:, c0:c0 + 128],
                                                 Eb[g % 2][:, c0:c0 + 128],
                                                 p.TRIU).then_inc(s_msk, 1)
                                v.wait_ge(s_pvg, wval(v, rw, i, 80,
                                                      640 * b + 40 * pp
                                                      + G_end[qc]))
                                v.reciprocal(recip, psL[:, :]) \
                                    .then_inc(s_rcp, 1)
                                v.wait_ge(s_rbc, wval(v, rw, i, 8,
                                                      64 * b + 4 * pp
                                                      + qc + 1))
                                v.tensor_copy(p.WF[:, 0:512], psR[:, :])
                                v.tensor_mul(sqb[:, qc * 512:(qc + 1) * 512],
                                             psOut[:, :], p.WF[:, 0:512]) \
                                    .then_inc(s_oq, 1)
                            v.tensor_copy(
                                p.HB[:, ds(axv(i, 2 * T, pp * T + b * 2048),
                                           2048)],
                                sqb[:, 0:2048]).then_inc(s_oev, 1)
                    body(0, True)
                    with fori(v, 1, 8) as i:
                        body(i, False)

        @blk.tensor
        def _(t):
            with Regs(t, 2) as (rw, r2):
                for b in range(2):
                    def body(i, first):
                        for pp in range(2):
                            t.wait_ge(s_ld, wval(t, rw, i, 32,
                                                 256 * b + 16 * pp + 16))
                            for half in range(2):
                                wge(t, s_vte, wval(t, rw, i, 4,
                                                   32 * b + 2 * pp + half))
                                for j8 in range(8):
                                    j = half * 8 + j8
                                    mm = t.transpose(psT[:, ts(j8, 128)],
                                                     vs[pp][:, ts(j, 128)],
                                                     p.IDENT)
                                mm.then_inc(s_vt, 1)

                            def emit_scores(g):
                                qc, j, c0 = groups[g]
                                wge(t, s_ex, wval(t, rw, i, 80,
                                                  640 * b + 40 * pp + g - 1))
                                t.matmul(psSc[g % 2][:, c0:512],
                                         ks[pp][:, ts(j, 128)],
                                         qs[pp][:, qc * 512 + c0:
                                                (qc + 1) * 512],
                                         start=True, stop=True) \
                                    .then_inc(s_sc, 1)

                            def emit_pv(g):
                                qc, j, c0 = groups[g]
                                jj = j - 4 * qc
                                t.wait_ge(s_ex, wval(t, rw, i, 80,
                                                     640 * b + 40 * pp
                                                     + g + 1))
                                if jj >= 0:
                                    ndb = 4 * qc + jj + 1
                                    t.wait_ge(s_msk, wval(
                                        t, rw, i, 32,
                                        256 * b + 16 * pp + ndb))
                                if j == 0:
                                    wge(t, s_oq, wval(t, rw, i, 8,
                                                      64 * b + 4 * pp + qc))
                                t.matmul(psL[:, c0:512], p.ONESC,
                                         Eb[g % 2][:, c0:512],
                                         start=(j == 0),
                                         stop=(j == 4 * qc + 3))
                                t.matmul(psOut[:, c0:512],
                                         vtok[:, ts(j, 128)],
                                         Eb[g % 2][:, c0:512],
                                         start=(j == 0),
                                         stop=(j == 4 * qc + 3)) \
                                    .then_inc(s_pvg, 1)
                                if j == 4 * qc + 3:
                                    t.wait_ge(s_rcp, wval(t, rw, i, 8,
                                                          64 * b + 4 * pp
                                                          + qc + 1))
                                    t.matmul(psR[:, :], p.ONES1F[:, :],
                                             recip, start=True, stop=True) \
                                        .then_inc(s_rbc, 1)

                            for g in range(NG):
                                emit_scores(g)
                                if g >= 1:
                                    emit_pv(g - 1)
                            emit_pv(NG - 1)
                    body(0, True)
                    with fori(t, 1, 8) as i:
                        body(i, False)


def phase_prep_mix(p):
    nc = p.nc
    s1, s2, s3 = p.dsem(), p.sem(), p.sem()
    ampb = p.WF[:, 0:16]
    esb = p.WF[:, 16:17]
    sq = p.WF[:, 32:48]
    pk = p.WF[:, 48:56]
    s_ = p.WF[:, 56:57]
    with nc.Block() as blk:
        @blk.gpsimd
        def _(g):
            g.dma_start(out=ampb, in_=p.amp[:, :]).then_inc(s1, 16)
            g.dma_start(out=esb, in_=p.esc[:, :]).then_inc(s1, 16)

        @blk.scalar
        def _(sc):
            sc.wait_ge(s1, 32)
            sc.activation(sq, ampb, AF.Square).then_inc(s2, 1)

        @blk.vector
        def _(v):
            st = mkchain(p, v)
            v.wait_ge(s2, 1)
            sq3 = sq.rearrange("p (k c) -> p k c", c=2)
            st(v.tensor_add(pk,
                            sq3[:, :, 0:1].rearrange("p k c -> p (k c)"),
                            sq3[:, :, 1:2].rearrange("p k c -> p (k c)")))
            st(v.reduce_sum(s_, pk, axis=AXX))
            st(v.tensor_scalar_add(s_, s_, 1e-8))
            st(v.reciprocal(s_, s_))
            st(v.tensor_scalar(pk, pk, s_, None, ALU.mult))
            v.tensor_scalar(p.MIXP, pk, esb, None, ALU.mult).then_inc(s3, 1)


CMB = 32768
CBC = 49152


def phase_ffn(p, res_in, res_out):
    fstep = int(os.environ.get("FSTEP", "9"))
    ths = range(2) if fstep >= 9 else range(1)
    for th in ths:
        p.semreset()
        phase_ln(p, res_in, col0=th * 2048, tw=2048)
        if fstep >= 2:
            p.semreset()
            ffn_router(p, th)
        if fstep >= 3:
            p.semreset()
            ffn_gateup(p, th)
        if fstep >= 4:
            p.semreset()
            ffn_shared(p, th)
        if fstep >= 5:
            p.semreset()
            ffn_down(p, res_in, res_out, th)
    if fstep < 9:
        nc = p.nc
        s_st = p.dsem()
        with nc.Block() as blk:
            @blk.gpsimd
            def _(g):
                for j in range(16):
                    g.dma_start(out=res_out[j * 128:(j + 1) * 128, :],
                                in_=p.HB[:, j * 4096:(j + 1) * 4096]) \
                        .then_inc(s_st, 16)
                g.dma_start(out=p.res1[0:128, 0:512],
                            in_=p.CONST[:, :]).then_inc(s_st, 16)
                g.wait_ge(s_st, 272)


def ffn_router(p, th):
    nc = p.nc
    rstep = int(os.environ.get("RSTEP", "9"))
    s_w = p.dsem()
    (s_pe, s_lg, s_sub, s_sc2, s_va, s_tb, s_kb,
     s_cb) = (p.sem() for _ in range(8))
    wslot = p.WK[:, 0:2048]
    lgT = p.WF[:, 0:2048]
    lt_all = p.WF[:, 2048:4096]
    prob_all = p.WF[:, 0:2048]
    cT = p.WK[0:8, 12288:14336]
    wf_all = p.WK[:, 14336:16384]
    wfT = p.WK[:, 16384:16512]

    with (
        nc.psum_tensor("pR" + _tag(), [128, 1024], F32) as pR,
        nc.psum_tensor("pF" + _tag(), [128, 1024], F32) as pF,
        nc.psum_tensor("pW" + _tag(), [128, 128], BF16) as pW,
        nc.psum_tensor("pCt" + _tag(), [8, 128], F32) as pCt,
        nc.psum_tensor("pK" + _tag(), [128, 1024], F32) as pK,
        nc.sbuf_tensor("RS" + _tag(), [128, 128], F32) as RS,
        nc.sbuf_tensor("RW" + _tag(), [128, 2048], F32) as RW,
        nc.Block() as blk,
    ):
        mxg = RS[:, 0:16]
        ssg = RS[:, 16:32]
        rcpg = RS[:, 32:48]
        msum = RS[:, 48:64]
        mtg = RS[:, 64:80]
        deng = RS[:, 80:96]
        work = RW[:, 0:2048]
        wmask = p.WK[:, 2048:4096]

        @blk.gpsimd
        def _(g):
            g.dma_start(out=wslot[:, :], in_=p.wrt[:, :]).then_inc(s_w, 16)

        @blk.tensor
        def _(t):
            t.wait_ge(s_w, 16)
            for q in range(4):
                wge(t, s_lg, q - 1)
                for dc in range(ND):
                    mm = t.matmul(pR[:, (q % 2) * 512:(q % 2) * 512 + 512],
                                  wslot[:, ts(dc, 128)],
                                  p.HB[:, dc * 2048 + q * 512:
                                       dc * 2048 + (q + 1) * 512],
                                  start=(dc == 0), stop=(dc == ND - 1))
                mm.then_inc(s_pe, 1)
            if rstep >= 2:
                for tt in range(16):
                    t.wait_ge(s_lg, (tt // 4) + 1)
                    wge(t, s_va, tt - 1)
                    t.matmul(pF[:, (tt % 2) * 512:(tt % 2) * 512 + 128],
                             lgT[:, ts(tt, 128)], p.CF[:, :],
                             start=True, stop=True).then_inc(s_tb, 1)
            if rstep >= 5:
                for tt in range(16):
                    t.wait_ge(s_va, 17 if tt == 0 else 2 * tt + 16)
                    t.transpose(pW[:, :], wf_all[:, ts(tt, 128)], p.IDENT) \
                        .then_inc(s_tb, 1)
                    t.wait_ge(s_va, 2 * tt + 18)
                    t.matmul(pCt[:, :], p.MIXP, wfT,
                             start=True, stop=True).then_inc(s_tb, 1)
            if rstep >= 6:
                t.wait_ge(s_va, 49)
                for k in range(8):
                    for cq in range(4):
                        g9 = 4 * k + cq
                        wge(t, s_cb, g9 - 1)
                        t.matmul(pK[:, (g9 % 2) * 512:(g9 % 2) * 512 + 512],
                                 p.C8[:, ts(k, 128)], cT[0:8, ts(cq, 512)],
                                 start=True, stop=True).then_inc(s_kb, 1)

        @blk.scalar
        def _(sc):
            if rstep >= 3:
                sc.wait_ge(s_sub, 1)
                sc.wait_ge(s_tb, 16)
                sc.activation(prob_all, lt_all, AF.Exp).then_inc(s_sc2, 1)

        @blk.vector
        def _(v):
            from concourse.bass import broadcast_tensor_aps
            st = mkchain(p, v)
            for q in range(4):
                v.wait_ge(s_pe, q + 1)
                v.tensor_copy(lgT[:, ts(q, 512)],
                              pR[:, (q % 2) * 512:(q % 2) * 512 + 512]) \
                    .then_inc(s_lg, 1)
            x3 = lt_all.rearrange("p (g e) -> p g e", g=16)
            if rstep >= 2:
                for tt in range(16):
                    v.wait_ge(s_tb, tt + 1)
                    v.tensor_copy(lt_all[:, ts(tt, 128)],
                                  pF[:, (tt % 2) * 512:(tt % 2) * 512 + 128]) \
                        .then_inc(s_va, 1)
            if rstep >= 3:
                st()
                mx3 = mxg.rearrange("p (g o) -> p g o", o=1)
                st(v.reduce_max(mxg, x3, axis=AXX))
                a3, b3 = broadcast_tensor_aps(x3, mx3)
                st(v.tensor_sub(x3, a3, b3))
                v.engine_nop().then_inc(s_sub, 1)
                v.wait_ge(s_sc2, 1)
                p3 = prob_all.rearrange("p (g e) -> p g e", g=16)
                st(v.reduce_sum(ssg, p3, axis=AXX))
                st(v.reciprocal(rcpg, ssg))
                r3 = rcpg.rearrange("p (g o) -> p g o", o=1)
                pa3, rb3 = broadcast_tensor_aps(p3, r3)
                st(v.tensor_mul(p3, pa3, rb3))
                st(v.tensor_copy(work, prob_all))
            if rstep >= 4:
                st(v.memset(wmask[:, :], 0.0))
                st(v.memset(msum, 0.0))
                w3 = work.rearrange("p (g e) -> p g e", g=16)
                wm3 = wmask.rearrange("p (g e) -> p g e", g=16)
                mt3 = mtg.rearrange("p (g o) -> p g o", o=1)
                for it in range(4):
                    st(v.reduce_max(mtg, w3, axis=AXX))
                    st(v.tensor_add(msum, msum, mtg))
                    wa3, mb3 = broadcast_tensor_aps(w3, mt3)
                    sel3 = x3
                    st(v.tensor_tensor(sel3, wa3, mb3, ALU.is_ge))
                    st(v.tensor_add(wm3, wm3, sel3))
                    st(v.scalar_tensor_tensor(w3, sel3, -2.0, w3,
                                              ALU.mult, ALU.add))
                st(v.tensor_scalar_add(deng, msum, 1e-8))
                st(v.reciprocal(deng, deng))
                d3 = deng.rearrange("p (g o) -> p g o", o=1)
                pa3b, db3 = broadcast_tensor_aps(p3, d3)
                st(v.tensor_mul(p3, pa3b, db3))
            if rstep >= 5:
                wf3 = wf_all.rearrange("p (g e) -> p g e", g=16)
                st(v.tensor_mul(wf3, p3, wm3))
                v.engine_nop().then_inc(s_va, 1)
                for tt in range(16):
                    v.wait_ge(s_tb, 16 + 2 * tt + 1)
                    v.tensor_copy(wfT, pW[:, :]).then_inc(s_va, 1)
                    v.wait_ge(s_tb, 16 + 2 * tt + 2)
                    v.tensor_copy(cT[0:8, ts(tt, 128)], pCt[:, :]) \
                        .then_inc(s_va, 1)
            if rstep >= 6:
                for k in range(8):
                    for cq in range(4):
                        g9 = 4 * k + cq
                        v.wait_ge(s_kb, g9 + 1)
                        v.tensor_copy(
                            p.HB[:, CBC + k * 2048 + cq * 512:
                                 CBC + k * 2048 + (cq + 1) * 512],
                            pK[:, (g9 % 2) * 512:(g9 % 2) * 512 + 512]) \
                            .then_inc(s_cb, 1)


def ffn_gateup(p, th):
    nc = p.nc
    s_w = p.dsem()
    s_pe, s_sl, s_b1 = (p.sem() for _ in range(3))
    wgu = [p.WK[:, 0:4096], p.WK[:, 4096:8192]]
    wgs = [w[:, 0:2048] for w in wgu]
    wus = [w[:, 2048:4096] for w in wgu]
    sg = [p.WK[:, 8192:9216], p.WK[:, 9216:10240]]
    tmp = [p.WK[:, 10240:11264], p.WK[:, 11264:12288]]
    t2 = p.WK[:, 12288:13312]

    with (
        nc.psum_tensor("psG" + _tag(), [128, 2048], F32) as psG,
        nc.psum_tensor("psU" + _tag(), [128, 2048], F32) as psU,
        nc.Block() as blk,
    ):
        pg = [psG[:, 0:1024], psG[:, 1024:2048]]
        pu = [psU[:, 0:1024], psU[:, 1024:2048]]

        @blk.gpsimd
        def _(g):
            for ic in range(64):
                wge(g, s_pe, 4 * ic - 4)
                g.dma_start(out=wgu[ic % 2][:, :],
                            in_=p.wgu[ic * 128:(ic + 1) * 128, :]) \
                    .then_inc(s_w, 16)

        @blk.tensor
        def _(t):
            with Regs(t, 1) as (rw,):
                def body(i, first):
                    for pp in range(2):
                        t.wait_ge(s_w, wval(t, rw, i, 32, 16 * pp + 16))
                        for q in range(2):
                            wge(t, s_sl, wval(t, rw, i, 4, 2 * pp + q - 1))
                            for dc in range(ND):
                                for c in range(2):
                                    mm = t.matmul(
                                        pg[q][:, c * 512:(c + 1) * 512],
                                        wgs[pp][:, ts(dc, 128)],
                                        p.HB[:, dc * 2048 + q * 1024
                                             + c * 512:
                                             dc * 2048 + q * 1024
                                             + (c + 1) * 512],
                                        start=(dc == 0), stop=(dc == ND - 1))
                            mm.then_inc(s_pe, 1)
                        for q in range(2):
                            wge(t, s_b1, wval(t, rw, i, 4, 2 * pp + q - 1))
                            for dc in range(ND):
                                for c in range(2):
                                    mm = t.matmul(
                                        pu[q][:, c * 512:(c + 1) * 512],
                                        wus[pp][:, ts(dc, 128)],
                                        p.HB[:, dc * 2048 + q * 1024
                                             + c * 512:
                                             dc * 2048 + q * 1024
                                             + (c + 1) * 512],
                                        start=(dc == 0), stop=(dc == ND - 1))
                            mm.then_inc(s_pe, 1)
                body(0, True)
                with fori(t, 1, 32) as i:
                    body(i, False)

        @blk.scalar
        def _(sc):
            with Regs(sc, 1) as (rw,):
                def body(i, first):
                    for pp in range(2):
                        for q in range(2):
                            sc.wait_ge(s_pe, wval(sc, rw, i, 8,
                                                  4 * pp + q + 1))
                            wge(sc, s_b1, wval(sc, rw, i, 4, 2 * pp + q - 1))
                            sc.activation(sg[q][:, :], pg[q][:, :],
                                          AF.Silu).then_inc(s_sl, 1)
                body(0, True)
                with fori(sc, 1, 32) as i:
                    body(i, False)

        @blk.vector
        def _(v):
            for ic in range(64):
                k = ic // 8
                for q in range(2):
                    v.wait_ge(s_sl, 2 * ic + q + 1)
                    v.wait_ge(s_pe, 4 * ic + 2 + q + 1)
                    v.scalar_tensor_tensor(tmp[q][:, :], pu[q][:, :], 1.0,
                                           sg[q][:, :], ALU.mult, ALU.mult) \
                        .then_inc(s_b1, 1)
                    cbcs = p.HB[:, CBC + k * 2048 + q * 1024:
                                CBC + k * 2048 + (q + 1) * 1024]
                    dsl = p.HB[:, CMB + (ic % 8) * 2048 + q * 1024:
                               CMB + (ic % 8) * 2048 + (q + 1) * 1024]
                    if k == 0:
                        v.tensor_mul(dsl, tmp[q][:, :], cbcs)
                    else:
                        v.tensor_mul(t2[:, :], tmp[q][:, :], cbcs)
                        v.tensor_add(dsl, dsl, t2[:, :])


def ffn_shared(p, th):
    nc = p.nc
    s_w, s_st = p.dsem(), p.dsem()
    s_pe, s_sl, s_b1 = (p.sem() for _ in range(3))
    wgs = [p.WK[:, 0:2048], p.WK[:, 2048:4096]]
    wus = [p.WK[:, 4096:6144], p.WK[:, 6144:8192]]
    sg = [p.WK[:, 8192:9216], p.WK[:, 9216:10240]]
    shs = [p.WK[:, 10240:12288], p.WK[:, 12288:14336]]

    with (
        nc.psum_tensor("psG" + _tag(), [128, 2048], F32) as psG,
        nc.psum_tensor("psU" + _tag(), [128, 2048], F32) as psU,
        nc.Block() as blk,
    ):
        pg = [psG[:, 0:1024], psG[:, 1024:2048]]
        pu = [psU[:, 0:1024], psU[:, 1024:2048]]

        @blk.gpsimd
        def _(g):
            for ic in range(16):
                pp = ic % 2
                wge(g, s_pe, 4 * ic - 4)
                g.dma_start(out=wgs[pp][:, :],
                            in_=p.swg[ic * 128:(ic + 1) * 128, :]) \
                    .then_inc(s_w, 16)
                g.dma_start(out=wus[pp][:, :],
                            in_=p.swu[ic * 128:(ic + 1) * 128, :]) \
                    .then_inc(s_w, 16)

        @blk.tensor
        def _(t):
            for ic in range(16):
                pp = ic % 2
                t.wait_ge(s_w, 32 * ic + 32)
                for q in range(2):
                    wge(t, s_sl, 2 * ic + q - 1)
                    for dc in range(ND):
                        for c in range(2):
                            mm = t.matmul(
                                pg[q][:, c * 512:(c + 1) * 512],
                                wgs[pp][:, ts(dc, 128)],
                                p.HB[:, dc * 2048 + q * 1024 + c * 512:
                                     dc * 2048 + q * 1024 + (c + 1) * 512],
                                start=(dc == 0), stop=(dc == ND - 1))
                    mm.then_inc(s_pe, 1)
                for q in range(2):
                    wge(t, s_b1, 2 * ic + q - 1)
                    for dc in range(ND):
                        for c in range(2):
                            mm = t.matmul(
                                pu[q][:, c * 512:(c + 1) * 512],
                                wus[pp][:, ts(dc, 128)],
                                p.HB[:, dc * 2048 + q * 1024 + c * 512:
                                     dc * 2048 + q * 1024 + (c + 1) * 512],
                                start=(dc == 0), stop=(dc == ND - 1))
                    mm.then_inc(s_pe, 1)

        @blk.scalar
        def _(sc):
            for ic in range(16):
                for q in range(2):
                    sc.wait_ge(s_pe, 4 * ic + q + 1)
                    wge(sc, s_b1, 2 * ic + q - 1)
                    sc.activation(sg[q][:, :], pg[q][:, :], AF.Silu) \
                        .then_inc(s_sl, 1)

        @blk.vector
        def _(v):
            for ic in range(16):
                pp = ic % 2
                for q in range(2):
                    v.wait_ge(s_sl, 2 * ic + q + 1)
                    v.wait_ge(s_pe, 4 * ic + 2 + q + 1)
                    wge(v, s_st, 16 * (ic - 1))
                    v.scalar_tensor_tensor(
                        shs[pp][:, q * 1024:(q + 1) * 1024],
                        pu[q][:, :], 1.0, sg[q][:, :],
                        ALU.mult, ALU.mult).then_inc(s_b1, 1)

        @blk.sync
        def _(sy):
            for ic in range(16):
                pp = ic % 2
                sy.wait_ge(s_b1, 2 * ic + 2)
                sy.dma_start(out=p.shT[ic * 128:(ic + 1) * 128, :],
                             in_=shs[pp][:, :]).then_inc(s_st, 16)
            sy.wait_ge(s_st, 16 * 16)


def ffn_down(p, res_in, res_out, th):
    nc = p.nc
    s_shl, s_w, s_st, s_x = (p.dsem() for _ in range(4))
    s_pe, s_ev = p.sem(), p.sem()
    wslots = [p.WK[:, 0:3072], p.WK[:, 3072:6144]]
    xslot = p.WF[:, 0:2048]
    eslot = p.WF[:, 2048:4096]
    ccs = [CMB + j * 2048 for j in range(8)] + [j * 2048 for j in range(16)]

    with (
        nc.psum_tensor("psA" + _tag(), [128, 2048], F32) as psA,
        nc.psum_tensor("psB" + _tag(), [128, 2048], F32) as psB,
        nc.Block() as blk,
    ):
        pq = [psA[:, 0:1024], psA[:, 1024:2048],
              psB[:, 0:1024], psB[:, 1024:2048]]

        @blk.gpsimd
        def _(g):
            for j in range(16):
                g.dma_start(out=p.HB[:, j * 2048:(j + 1) * 2048],
                            in_=p.shT[j * 128:(j + 1) * 128, :]) \
                    .then_inc(s_shl, 16)
            for oc in range(16):
                wge(g, s_pe, 2 * oc - 2)
                g.dma_start(out=wslots[oc % 2][:, :],
                            in_=p.wds[oc * 128:(oc + 1) * 128, :]) \
                    .then_inc(s_w, 16)

        @blk.tensor
        def _(t):
            t.wait_ge(s_shl, 256)
            with Regs(t, 1) as (rw,):
                def body(i, first):
                    for pp in range(2):
                        t.wait_ge(s_w, wval(t, rw, i, 32, 16 * pp + 16))
                        for q in range(2):
                            wge(t, s_ev, wval(t, rw, i, 4, 2 * pp + q - 3))
                            for cc in range(24):
                                for c in range(2):
                                    mm = t.matmul(
                                        pq[2 * pp + q][:,
                                                       c * 512:(c + 1) * 512],
                                        wslots[pp][:, ts(cc, 128)],
                                        p.HB[:, ccs[cc] + q * 1024 + c * 512:
                                             ccs[cc] + q * 1024
                                             + (c + 1) * 512],
                                        start=(cc == 0), stop=(cc == 23))
                            mm.then_inc(s_pe, 1)
                body(0, True)
                with fori(t, 1, 8) as i:
                    body(i, False)

        @blk.vector
        def _(v):
            for oc in range(16):
                pp = oc % 2
                for q in range(2):
                    gq = 2 * oc + q
                    v.wait_ge(s_pe, gq + 1)
                    v.wait_ge(s_x, 16 * oc + 16)
                    wge(v, s_st, 16 * oc)
                    v.tensor_add(eslot[:, q * 1024:(q + 1) * 1024],
                                 pq[2 * pp + q][:, :],
                                 xslot[:, q * 1024:(q + 1) * 1024]) \
                        .then_inc(s_ev, 1)

        @blk.sync
        def _(sy):
            for oc in range(16):
                wge(sy, s_ev, 2 * oc)
                sy.dma_start(out=xslot[:, :],
                             in_=res_in[oc * 128:(oc + 1) * 128,
                                        th * 2048:(th + 1) * 2048]) \
                    .then_inc(s_x, 16)
                sy.wait_ge(s_ev, 2 * oc + 2)
                sy.dma_start(out=res_out[oc * 128:(oc + 1) * 128,
                                         th * 2048:(th + 1) * 2048],
                             in_=eslot[:, :]).then_inc(s_st, 16)
            sy.wait_ge(s_st, 16 * 16)


def phase_attn_b(p):
    nc = p.nc
    s_ld = p.dsem()
    (s_vt, s_vte, s_sc, s_ex, s_msk, s_pvg, s_rcp, s_rbc,
     s_oq, s_oev) = (p.sem() for _ in range(10))
    ks = [p.WK[:, 0:2048], p.WK[:, 2048:4096]]
    vs = [p.WK[:, 4096:6144], p.WK[:, 6144:8192]]
    qs = [[p.WK[:, 8192:10240], p.WK[:, 10240:12288]],
          [p.WK[:, 12288:14336], p.WK[:, 14336:16384]]]
    vtok = p.WK[:, 16384:17424]
    vtok3 = vtok.rearrange("p (t c) -> p t c", c=65)
    Eb = [p.WK[:, 17424:17936], p.WK[:, 17936:18448]]
    recip = p.R[0:1, 0:512]
    rbc = p.WF[:, 0:512]
    oq = p.WF[:, 512:2560]
    ID64 = p.IDENT[0:64, 0:64]

    groups = []
    for qc in range(4):
        for j2 in range(4 * qc + 4):
            jj = j2 - 4 * qc
            groups.append((qc, j2, 128 * jj if jj >= 0 else 0))
    NG = len(groups)
    G_end = [4, 12, 24, 40]

    with (
        nc.psum_tensor("psS0b" + _tag(), [128, 512], F32) as psS0,
        nc.psum_tensor("psS1b" + _tag(), [128, 512], F32) as psS1,
        nc.psum_tensor("psOb" + _tag(), [65, 512], F32) as psOut,
        nc.psum_tensor("psTb" + _tag(), [128, 1024], BF16) as psT4,
        nc.psum_tensor("psRb" + _tag(), [64, 512], F32) as psR4,
        nc.Block() as blk,
    ):
        psSc = [psS0, psS1]

        @blk.sync
        def _(sy):
            with Regs(sy, 1) as (rw,):
                for b in range(2):
                    def body(jp, first):
                        wge(sy, s_oq, wval(sy, rw, jp, 32, 128 * b))
                        sy.dma_start(
                            out=p.WK[0:64, 0:4096].rearrange(
                                "p (s t) -> p s t", s=2),
                            in_=p.qkvA[ds(axv(jp, 128, 2048), 128),
                                       b * 2048:(b + 1) * 2048].rearrange(
                                "(s p) t -> p s t", p=64)) \
                            .then_inc(s_ld, 16)
                        sy.dma_start(
                            out=p.WK[64:128, 0:4096].rearrange(
                                "p (s t) -> p s t", s=2),
                            in_=p.qkvA[ds(axv(jp, 128, 2048), 128),
                                       b * 2048:(b + 1) * 2048].rearrange(
                                "(s p) t -> p s t", p=64)) \
                            .then_inc(s_ld, 16)
                        sy.dma_start(
                            out=p.WK[0:64, 4096:8192].rearrange(
                                "p (s t) -> p s t", s=2),
                            in_=p.qkvA[ds(axv(jp, 128, 2560), 128),
                                       b * 2048:(b + 1) * 2048].rearrange(
                                "(s p) t -> p s t", p=64)) \
                            .then_inc(s_ld, 16)
                        sy.dma_start(
                            out=p.WK[:, 8192:16384].rearrange(
                                "p (q t) -> p q t", q=4),
                            in_=p.qkvA[ds(axv(jp, 512, 0), 512),
                                       b * 2048:(b + 1) * 2048].rearrange(
                                "(q p) t -> p q t", p=128)) \
                            .then_inc(s_ld, 16)
                    body(0, True)
                    with fori(sy, 1, 4) as jp:
                        body(jp, False)

        @blk.scalar
        def _(sc):
            with Regs(sc, 1) as (rw,):
                for b in range(2):
                    def body(jp, first):
                        for s in range(2):
                            for u in range(4):
                                mb = 1280 * b + 160 * s + 40 * u
                                for g in range(NG):
                                    sc.wait_ge(s_sc, wval(sc, rw, jp, 320,
                                                          mb + g + 1))
                                    wge(sc, s_pvg, wval(sc, rw, jp, 320,
                                                        mb + g - 1))
                                    qc, j2, c0 = groups[g]
                                    sc.activation(Eb[g % 2][:, c0:512],
                                                  psSc[g % 2][:, c0:512],
                                                  AF.Exp, scale=0.125) \
                                        .then_inc(s_ex, 1)
                    body(0, True)
                    with fori(sc, 1, 4) as jp:
                        body(jp, False)

        @blk.vector
        def _(v):
            v.memset(vtok3[:, :, 64:65], 1.0)
            with Regs(v, 1) as (rw,):
                for b in range(2):
                    def body(jp, first):
                        for s in range(2):
                            v.wait_ge(s_vt, wval(v, rw, jp, 2,
                                                 8 * b + s + 1))
                            wge(v, s_pvg, wval(v, rw, jp, 320,
                                               1280 * b + 160 * s))
                            v.tensor_copy(
                                vtok3[:, :, 0:64],
                                psT4[:, :].rearrange("p (t c) -> p t c",
                                                     c=64)) \
                                .then_inc(s_vte, 1)
                            for u in range(4):
                                mb = 1280 * b + 160 * s + 40 * u
                                m4 = 128 * b + 16 * s + 4 * u
                                pu = u // 2
                                qrow = 64 * (u % 2)
                                if u % 2 == 0:
                                    wge(v, s_oev, wval(
                                        v, rw, jp, 4,
                                        16 * b + 2 * s + 2 * pu - 1))
                                for qc in range(4):
                                    for g in range(G_end[qc] - 4,
                                                   G_end[qc]):
                                        _qc, j2, c0 = groups[g]
                                        v.wait_ge(s_ex, wval(
                                            v, rw, jp, 320, mb + g + 1))
                                        v.tensor_mul(
                                            Eb[g % 2][:, c0:c0 + 128],
                                            Eb[g % 2][:, c0:c0 + 128],
                                            p.TRIU).then_inc(s_msk, 1)
                                    v.wait_ge(s_pvg, wval(
                                        v, rw, jp, 320, mb + G_end[qc]))
                                    v.reciprocal(recip,
                                                 psOut[64:65, :]) \
                                        .then_inc(s_rcp, 1)
                                    v.wait_ge(s_rbc, wval(
                                        v, rw, jp, 32, m4 + qc + 1))
                                    v.tensor_copy(rbc[0:64, :], psR4[:, :])
                                    v.tensor_mul(
                                        oq[qrow:qrow + 64, ts(qc, 512)],
                                        psOut[0:64, :], rbc[0:64, :]) \
                                        .then_inc(s_oq, 1)
                                if u % 2 == 1:
                                    v.tensor_copy(
                                        p.HB[:, ds(axv(jp, 4 * T,
                                                       (2 * s + pu) * T
                                                       + b * 2048), 2048)],
                                        oq[:, :]).then_inc(s_oev, 1)
                    body(0, True)
                    with fori(v, 1, 4) as jp:
                        body(jp, False)

        @blk.tensor
        def _(t):
            with Regs(t, 1) as (rw,):
                for b in range(2):
                    def body(jp, first):
                        for s in range(2):
                            t.wait_ge(s_ld, wval(t, rw, jp, 64,
                                                 256 * b + 64))
                            wge(t, s_vte, wval(t, rw, jp, 2, 8 * b + s))
                            for c in range(16):
                                mm = t.transpose(psT4[:, ts(c, 64)],
                                                 vs[s][0:64, ts(c, 128)],
                                                 ID64)
                            mm.then_inc(s_vt, 1)
                            for u in range(4):
                                mb = 1280 * b + 160 * s + 40 * u
                                m4 = 128 * b + 16 * s + 4 * u
                                qrow = 64 * (u % 2)
                                hq = qs[s][u // 2]

                                def emit_scores(g):
                                    qc, j2, c0 = groups[g]
                                    wge(t, s_ex, wval(t, rw, jp, 320,
                                                      mb + g - 1))
                                    t.matmul(
                                        psSc[g % 2][:, c0:512],
                                        ks[s][qrow:qrow + 64, ts(j2, 128)],
                                        hq[qrow:qrow + 64,
                                           qc * 512 + c0:(qc + 1) * 512],
                                        start=True, stop=True) \
                                        .then_inc(s_sc, 1)

                                def emit_pv(g):
                                    qc, j2, c0 = groups[g]
                                    jj = j2 - 4 * qc
                                    t.wait_ge(s_ex, wval(t, rw, jp, 320,
                                                         mb + g + 1))
                                    if jj >= 0:
                                        ndb = 4 * qc + jj + 1
                                        t.wait_ge(s_msk, wval(
                                            t, rw, jp, 128,
                                            512 * b + 64 * s + 16 * u
                                            + ndb))
                                    if j2 == 0:
                                        wge(t, s_oq, wval(t, rw, jp, 32,
                                                          m4 + qc))
                                        if qc == 0:
                                            t.wait_ge(s_vte, wval(
                                                t, rw, jp, 2,
                                                8 * b + s + 1))
                                    t.matmul(psOut[:, c0:512],
                                             vtok[:, j2 * 65:j2 * 65 + 65],
                                             Eb[g % 2][:, c0:512],
                                             start=(j2 == 0),
                                             stop=(j2 == 4 * qc + 3)) \
                                        .then_inc(s_pvg, 1)
                                    if j2 == 4 * qc + 3:
                                        t.wait_ge(s_rcp, wval(
                                            t, rw, jp, 32, m4 + qc + 1))
                                        t.matmul(psR4[:, :], p.C99,
                                                 recip, start=True,
                                                 stop=True) \
                                            .then_inc(s_rbc, 1)

                                for g in range(NG):
                                    emit_scores(g)
                                    if g >= 1:
                                        emit_pv(g - 1)
                                emit_pv(NG - 1)
                    body(0, True)
                    with fori(t, 1, 4) as jp:
                        body(jp, False)


def phase_bprep(p):
    nc = p.nc
    s_ld, s_st = p.dsem(), p.dsem()
    s_e1, s_sm, s_rc, s_bc2, s_qf, s_gf, s_sg = (p.sem() for _ in range(7))
    rk = [p.WK[:, 0:4096], p.WK[:, 4096:8192]]
    ex = [p.WK[:, 8192:12288], p.WK[:, 12288:16384]]
    rcb = [p.WK[0:2, 16384:16896], p.WK[0:2, 16896:17408]]
    aa = p.WF[:, 0:2048]
    ee = p.WF[:, 2048:4096]
    NCH = 20

    def nst(ch):
        return 0 if ch < 0 else ((ch + 1) if ch <= 15 else 16 + (ch - 15) * 3)

    with (
        nc.psum_tensor("pS2" + _tag(), [2, 1024], F32) as pS2,
        nc.psum_tensor("pB" + _tag(), [128, 1024], F32) as pB,
        nc.Block() as blk,
    ):
        @blk.gpsimd
        def _(g):
            for ch in range(NCH):
                row = ch * 128 if ch <= 15 else 2048 + (ch - 16) * 128
                pv = ch - 2
                if pv >= 0:
                    if pv <= 15:
                        g.wait_ge(s_e1, pv + 1)
                    else:
                        g.wait_ge(s_st, 16 * nst(pv))
                g.dma_start(out=rk[ch % 2][:, :],
                            in_=p.qkvA[row:row + 128, :]).then_inc(s_ld, 16)

        @blk.scalar
        def _(sc):
            for ch in range(NCH):
                sc.wait_ge(s_ld, 16 * (ch + 1))
                if ch >= 2:
                    sc.wait_ge(s_st, 16 * nst(ch - 2))
                sc.activation(ex[ch % 2][:, :], rk[ch % 2][:, :],
                              AF.Exp).then_inc(s_e1, 1)
                if ch >= 16:
                    kc = ch - 16
                    for hf in range(2):
                        wge(sc, s_gf, 2 * kc + hf)
                        sl = rk[ch % 2][:, hf * 2048:(hf + 1) * 2048]
                        sc.activation(aa, sl, AF.Abs)
                        sc.activation(ee, aa, AF.Exp, scale=-1.0)
                        sc.activation(aa, ee, AF.Ln, bias=p.FONE) \
                            .then_inc(s_sg, 1)

        @blk.tensor
        def _(t):
            def emit_bc(ch, c):
                t.wait_ge(s_rc, 8 * ch + c + 1)
                wge(t, s_qf, 8 * ch + c - 1)
                t.matmul(pB[:, (c % 2) * 512:(c % 2) * 512 + 512],
                         p.BD2T, rcb[c % 2][:, :],
                         start=True, stop=True).then_inc(s_bc2, 1)

            for ch in range(NCH):
                t.wait_ge(s_e1, ch + 1)
                for c in range(8):
                    wge(t, s_rc, 8 * ch + c - 1)
                    t.matmul(pS2[:, (c % 2) * 512:(c % 2) * 512 + 512],
                             p.BD2, ex[ch % 2][:, ts(c, 512)],
                             start=True, stop=True).then_inc(s_sm, 1)
                    if c >= 1:
                        emit_bc(ch, c - 1)
                emit_bc(ch, 7)

        @blk.vector
        def _(v):
            for ch in range(NCH):
                for c in range(8):
                    v.wait_ge(s_sm, 8 * ch + c + 1)
                    wge(v, s_bc2, 8 * ch + c - 1)
                    with nc.allow_low_precision(
                            reason="bf16 softmax-denominator recips feed the"
                                   " 1%-weighted fla branch only"):
                        v.reciprocal(rcb[c % 2][:, :],
                                     pS2[:, (c % 2) * 512:
                                         (c % 2) * 512 + 512]) \
                            .then_inc(s_rc, 1)
                    v.wait_ge(s_bc2, 8 * ch + c + 1)
                    v.tensor_mul(ex[ch % 2][:, ts(c, 512)],
                                 ex[ch % 2][:, ts(c, 512)],
                                 pB[:, (c % 2) * 512:(c % 2) * 512 + 512]) \
                        .then_inc(s_qf, 1)
                if ch >= 16:
                    kc = ch - 16
                    for hf in range(2):
                        v.wait_ge(s_sg, 2 * kc + hf + 1)
                        sl = rk[ch % 2][:, hf * 2048:(hf + 1) * 2048]
                        v.scalar_tensor_tensor(sl, sl, 0.0, aa,
                                               ALU.min, ALU.subtract) \
                            .then_inc(s_gf, 1)

        @blk.sync
        def _(sy):
            for ch in range(NCH):
                sy.wait_ge(s_qf, 8 * (ch + 1))
                if ch <= 15:
                    dst = p.qfT[ch * 128:(ch + 1) * 128, :]
                else:
                    dst = p.qkvA[5120 + (ch - 16) * 128:
                                 5120 + (ch - 15) * 128, :]
                sy.dma_start(out=dst, in_=ex[ch % 2][:, :]).then_inc(s_st, 16)
                if ch >= 16:
                    kc = ch - 16
                    for hf in range(2):
                        sy.wait_ge(s_gf, 2 * kc + hf + 1)
                        sy.dma_start(
                            out=p.qkvA[5632 + kc * 128:5632 + (kc + 1) * 128,
                                       hf * 2048:(hf + 1) * 2048],
                            in_=rk[ch % 2][:, hf * 2048:(hf + 1) * 2048]) \
                            .then_inc(s_st, 16)
            sy.wait_ge(s_st, 16 * nst(NCH - 1))


def phase_fla(p):
    nc = p.nc
    s_fld, s_fst = p.dsem(), p.dsem()
    (s_b2, s_sce, s_kq, s_pp, s_pm, s_vt5, s_vev, s_kt5, s_kev,
     s_och, s_su, s_sch, s_ofd) = (p.sem() for _ in range(13))
    kfs = p.WK[:, 0:2048]
    vfs = p.WK[:, 2048:4096]
    qfa = p.WK[:, 4096:12288]
    expBq = p.WK[:, 12288:12416]
    enB = p.WK[:, 12416:12544]
    enC = p.WK[:, 12544:12672]
    qta = p.WK[:, 12672:13184]
    kt = p.WK[:, 13184:13312]
    kh = p.WK[:, 13312:13440]
    vtok = p.WK[:, 13440:13504]
    khtok = p.WK[:, 13504:13568]
    Pm = p.WK[:, 13568:14080]
    S16 = p.WK[:, 14080:14144]
    gf2b = p.WK[:, 14144:16192]
    gfs = p.WF[:, 0:2048]
    B2 = p.WF[:, 2048:2176]
    zer = p.WF[:, 2176:2304]
    Ssb = p.WF[:, 2368:2432]
    ofa = p.HB[:, 0:8192]
    ID64 = p.IDENT[0:64, 0:64]
    qfa3 = qfa.rearrange("p (u t) -> p u t", u=4)
    ofa3 = ofa.rearrange("p (u t) -> p u t", u=4)
    from concourse.bass import broadcast_tensor_aps

    with (
        nc.psum_tensor("pP5" + _tag(), [128, 512], F32) as pP,
        nc.psum_tensor("pO5" + _tag(), [64, 512], F32) as pO,
        nc.psum_tensor("pTv5" + _tag(), [128, 64], BF16) as pTv,
        nc.psum_tensor("pTk5" + _tag(), [128, 64], BF16) as pTk,
        nc.psum_tensor("pSu5" + _tag(), [64, 64], F32) as pSu,
        nc.Block() as blk,
    ):
        @blk.gpsimd
        def _(g):
            g.memset(zer[:, :], 0.0)
            with Regs(g, 1) as (rw,):
                for b in range(2):
                    def body(j, first):
                        bj = axv(j, 1, 8 * b)
                        g.dma_start(
                            out=kfs[0:64, :],
                            in_=p.qkvA[ds(axv(j, 64, 5120), 64),
                                       b * 2048:(b + 1) * 2048]) \
                            .then_inc(s_fld, 16)
                        g.dma_start(
                            out=vfs[0:64, :],
                            in_=p.qkvA[ds(axv(j, 64, 2560), 64),
                                       b * 2048:(b + 1) * 2048]) \
                            .then_inc(s_fld, 16)
                        for half in range(2):
                            g.dma_start(
                                out=gf2b[64 * half:64 * half + 64, :],
                                in_=p.qkvA[ds(axv(j, 64, 5632), 64),
                                           b * 2048:(b + 1) * 2048]) \
                                .then_inc(s_fld, 16)
                        g.dma_start(
                            out=qfa[0:64, :].rearrange(
                                "p (u t) -> p u t", u=4),
                            in_=p.qfT[ds(axv(j, 256, 0), 256),
                                      b * 2048:(b + 1) * 2048].rearrange(
                                "(u p) t -> p u t", p=64)) \
                            .then_inc(s_fld, 16)
                        g.wait_ge(s_ofd, wval(g, rw, j, 16, 128 * b + 16))
                        g.dma_start(
                            out=p.flaT[ds(axv(j, 256, 0), 256),
                                       b * 2048:(b + 1) * 2048].rearrange(
                                "(u p) t -> p u t", p=64),
                            in_=ofa[0:64, :].rearrange(
                                "p (u t) -> p u t", u=4)) \
                            .then_inc(s_fst, 16)
                    body(0, True)
                    with fori(g, 1, 8) as j:
                        body(j, False)

        @blk.scalar
        def _(sc):
            with Regs(sc, 1) as (rw,):
                for b in range(2):
                    def body(j, first):
                        for c in range(16):
                            sc.wait_ge(s_b2, wval(sc, rw, j, 16,
                                                  128 * b + c + 1))
                            sc.activation(expBq[:, :], B2, AF.Exp,
                                          bias=p.FLNC)
                            sc.activation(enB[:, :], B2, AF.Exp, scale=-1.0)
                            sc.activation(enC[:, :], B2, AF.Exp, scale=-1.0,
                                          bias=B2[:, 127:128])
                            sc.activation(p.WF[:, 2304:2305], B2[:, 127:128],
                                          AF.Exp).then_inc(s_sce, 1)
                    body(0, True)
                    with fori(sc, 1, 8) as j:
                        body(j, False)

        @blk.vector
        def _(v):
            with Regs(v, 1) as (rw,):
                for b in range(2):
                    def body(j, first):
                        bj = 128 * b
                        v.wait_ge(s_fld, wval(v, rw, j, 80, 640 * b + 80))
                        wge(v, s_fst, wval(v, rw, j, 16, 128 * b))
                        v.memset(Ssb[0:64, :], 0.0).then_inc(s_sch, 1)
                        v.memset(S16[0:64, :], 0.0).then_inc(s_sch, 1)
                        v.tensor_scalar_mul(gfs, gf2b, 1.0 / 16.0) \
                            .then_inc(s_sch, 1)
                        for c in range(16):
                            wge(v, s_sce, wval(v, rw, j, 16, bj + c))
                            v.tensor_tensor_scan(B2, gfs[:, ts(c, 128)],
                                                 zer, 0.0,
                                                 ALU.add, ALU.add) \
                                .then_inc(s_b2, 1)
                            v.wait_ge(s_sce, wval(v, rw, j, 16, bj + c + 1))
                            qv = qfa3[0:64, :, c * 128:(c + 1) * 128]
                            eb = expBq[0:64, :].rearrange(
                                "p (o t) -> p o t", o=1)
                            a3, b3 = broadcast_tensor_aps(qv, eb)
                            v.tensor_mul(
                                qta[0:64, :].rearrange(
                                    "p (u t) -> p u t", u=4), a3, b3)
                            v.tensor_mul(kt[0:64, :], kfs[0:64, ts(c, 128)],
                                         enB[0:64, :])
                            v.tensor_mul(kh[0:64, :], kfs[0:64, ts(c, 128)],
                                         enC[0:64, :]).then_inc(s_kq, 1)
                            v.wait_ge(s_vt5, wval(v, rw, j, 16, bj + c + 1))
                            wge(v, s_su, wval(v, rw, j, 16, bj + c))
                            v.tensor_copy(vtok[:, :], pTv[:, :]) \
                                .then_inc(s_vev, 1)
                            v.wait_ge(s_kt5, wval(v, rw, j, 16, bj + c + 1))
                            v.tensor_copy(khtok[:, :], pTk[:, :]) \
                                .then_inc(s_kev, 1)
                            v.wait_ge(s_pp, wval(v, rw, j, 16, bj + c + 1))
                            v.tensor_mul(Pm[:, :], pP[:, :], p.TRIU4) \
                                .then_inc(s_pm, 1)
                            v.wait_ge(s_och, wval(v, rw, j, 16, bj + c + 1))
                            v.tensor_copy(
                                ofa3[0:64, :, c * 128:(c + 1) * 128],
                                pO[:, :].rearrange("p (u t) -> p u t", u=4)) \
                                .then_inc(s_ofd, 1)
                            sb = 51 * 8 * b
                            v.tensor_scalar(Ssb[0:64, :], Ssb[0:64, :],
                                            p.WF[0:64, 2304:2305], None,
                                            ALU.mult).then_inc(s_sch, 1)
                            v.wait_ge(s_sch, wval(v, rw, j, 51,
                                                  sb + 3 * c + 4))
                            v.wait_ge(s_su, wval(v, rw, j, 16, bj + c + 1))
                            v.tensor_add(Ssb[0:64, :], Ssb[0:64, :],
                                         pSu[:, :]).then_inc(s_sch, 1)
                            v.wait_ge(s_sch, wval(v, rw, j, 51,
                                                  sb + 3 * c + 5))
                            v.tensor_copy(S16[0:64, :], Ssb[0:64, :]) \
                                .then_inc(s_sch, 1)
                    body(0, True)
                    with fori(v, 1, 8) as j:
                        body(j, False)

        @blk.tensor
        def _(t):
            with Regs(t, 1) as (rw,):
                for b in range(2):
                    def body(j, first):
                        bj = 128 * b
                        sb = 51 * 8 * b
                        for c in range(16):
                            wge(t, s_vev, wval(t, rw, j, 16, bj + c))
                            t.wait_ge(s_fld, wval(t, rw, j, 80,
                                                  640 * b + 80))
                            t.transpose(pTv[:, :], vfs[0:64, ts(c, 128)],
                                        ID64).then_inc(s_vt5, 1)
                            t.wait_ge(s_kq, wval(t, rw, j, 16, bj + c + 1))
                            wge(t, s_pm, wval(t, rw, j, 16, bj + c))
                            t.matmul(pP[:, :], kt[0:64, :], qta[0:64, :],
                                     start=True, stop=True).then_inc(s_pp, 1)
                            wge(t, s_kev, wval(t, rw, j, 16, bj + c))
                            t.transpose(pTk[:, :], kh[0:64, :], ID64) \
                                .then_inc(s_kt5, 1)
                            t.wait_ge(s_pm, wval(t, rw, j, 16, bj + c + 1))
                            t.wait_ge(s_vev, wval(t, rw, j, 16, bj + c + 1))
                            wge(t, s_ofd, wval(t, rw, j, 16, bj + c))
                            t.matmul(pO[:, :], vtok[:, :], Pm[:, :],
                                     start=True, stop=False)
                            t.wait_ge(s_sch, wval(t, rw, j, 51,
                                                  sb + 3 * c + 3))
                            t.matmul(pO[:, :], S16[0:64, :], qta[0:64, :],
                                     start=False, stop=True) \
                                .then_inc(s_och, 1)
                            t.wait_ge(s_kev, wval(t, rw, j, 16, bj + c + 1))
                            wge(t, s_sch, wval(t, rw, j, 51, sb + 3 * c + 2))
                            t.matmul(pSu[:, :], khtok[:, :], vtok[:, :],
                                     start=True, stop=True).then_inc(s_su, 1)
                    body(0, True)
                    with fori(t, 1, 8) as j:
                        body(j, False)


def host_layoutA(w):
    Din, O = w.shape
    nd, no = Din // 128, O // 128
    return np.ascontiguousarray(
        w.reshape(nd, 128, no, 128).transpose(2, 1, 0, 3)
        .reshape(no * 128, nd * 128))


def host_inputs(inp):
    f32 = lambda k: np.asarray(inp[k], np.float32)
    wq = f32("L0_attn_a_w_qkv")
    wq = wq.reshape(3, 16, 128, 2048).transpose(1, 0, 2, 3).reshape(6144, 2048)
    d = {
        "wqkv": host_layoutA(np.ascontiguousarray(wq.T)),
        "wo_a": host_layoutA(np.ascontiguousarray(f32("L0_attn_a_w_o").T)),
        "wrt": host_layoutA(f32("L0_ffn_w_router")),
        "wgu": np.ascontiguousarray(np.concatenate(
            [host_layoutA(f32("L0_ffn_w_gate")),
             host_layoutA(f32("L0_ffn_w_up"))], axis=1)),
        "wds": np.ascontiguousarray(np.concatenate(
            [host_layoutA(f32("L0_ffn_w_down")),
             host_layoutA(f32("L0_ffn_shared_w_down"))], axis=1)),
        "swg": host_layoutA(f32("L0_ffn_shared_w_gate")),
        "swu": host_layoutA(f32("L0_ffn_shared_w_up")),
        "wqkvB": host_layoutA(np.ascontiguousarray(np.concatenate(
            [f32("L0_attn_b_Wq"), f32("L0_attn_b_Wk"),
             f32("L0_attn_b_Wv")], axis=1))),
        "wo_b": host_layoutA(f32("L0_attn_b_Wo")),
        "amp": np.ascontiguousarray(
            f32("L0_ffn_amplitudes").reshape(128, 16)),
        "esc": np.ascontiguousarray(
            f32("L0_ffn_expert_scale").reshape(128, 1)),
    }
    return d


def kernel(**inp):
    h = np.asarray(inp["L0_attn_a_hidden_states"], dtype=np.float32)
    wd = host_inputs(inp)
    p = build_program()
    in_maps = []
    for c in range(8):
        hc = h[2 * c:2 * c + 2]
        xT = np.ascontiguousarray(hc.transpose(2, 0, 1).reshape(D, T))
        m = dict(wd)
        m["xT"] = xT
        in_maps.append(m)
    res = run_bass_kernel_spmd(p.nc, in_maps, core_ids=list(range(8)))
    global _dbg_res1
    _dbg_res1 = np.asarray(res.results[0]["res1"])
    nph = int(os.environ.get("KPHASES", "99"))
    key = ("res4" if nph >= 10 else "res3" if nph >= 9 else
           "res2" if nph >= 5 else "res1")
    outs = []
    for c in range(8):
        rT = np.asarray(res.results[c][key])
        outs.append(rT.reshape(D, B, S).transpose(1, 2, 0))
    return np.concatenate(outs, axis=0)
